# version 04 (final)
# baseline (speedup 1.0000x reference)
import numpy as np
import ml_dtypes
import concourse.bass as bass
import concourse.mybir as mybir
from concourse.bass_utils import run_bass_kernel_spmd
from contextlib import ExitStack

F32 = mybir.dt.float32
BF16 = mybir.dt.bfloat16
U8 = mybir.dt.uint8
AF = mybir.ActivationFunctionType
ALU = mybir.AluOpType
AX = mybir.AxisListType

SEQ = 2048
E = 2048
NT = 16
KT = 16


class Bld:
    def __init__(self):
        self.q = {s: [] for s in ('ld', 'wld', 'pe', 'act', 'dve')}
        self.cnt = {s: 0 for s in ('ld', 'wld', 'pe', 'act', 'dve', 'st')}

    def _emit(self, stream, kind, payload, deps):
        sem = 'st' if kind == 'st' else ('wld' if kind == 'wld2' else
                                         ('ld' if kind == 'ldq' else stream))
        self.cnt[sem] += 1
        tok = (sem, self.cnt[sem])
        self.q[stream].append((kind, payload, tuple(deps), tok))
        return tok

    def op(self, stream, fn, deps=()):
        return self._emit(stream, 'op', fn, deps)

    def load(self, dst, src, deps=(), q='ld'):
        return self._emit('ld', 'ldq', (dst, src), deps)

    def store(self, dst, src, deps=()):
        return self._emit('act', 'st', (dst, src), deps)

    def mm(self, out, lhsT, rhs, start, stop, deps=()):
        return self.op('pe', lambda e, o=out, l=lhsT, r=rhs, s1=start, s2=stop:
                       e.matmul(o, l, r, start=s1, stop=s2, skip_group_check=True), deps)

    def trans(self, out, in_, ident, deps=()):
        return self.op('pe', lambda e, o=out, i=in_, d=ident: e.transpose(o, i, d), deps)

    def act(self, out, in_, func, deps=(), bias=None, scale=None):
        kw = {}
        if bias is not None:
            kw['bias'] = bias
        if scale is not None:
            kw['scale'] = scale
        return self.op('act', lambda e, o=out, i=in_, f=func, k=kw:
                       e.activation(out=o, in_=i, func=f, **k), deps)

    def tt(self, out, a, c, alu, deps=(), eng='dve'):
        return self.op(eng, lambda e, o=out, x=a, y=c, p=alu:
                       e.tensor_tensor(out=o, in0=x, in1=y, op=p), deps)

    def ts(self, out, a, s1, alu, deps=(), eng='dve'):
        return self.op(eng, lambda e, o=out, x=a, s=s1, p=alu:
                       e.tensor_scalar(out=o, in0=x, scalar1=s, scalar2=0.0,
                                       op0=p, op1=ALU.add), deps)

    def ts2(self, out, a, s1, alu1, s2, alu2, deps=(), eng='dve'):
        return self.op(eng, lambda e, o=out, x=a, sa=s1, p1=alu1, sb=s2, p2=alu2:
                       e.tensor_scalar(out=o, in0=x, scalar1=sa, scalar2=sb,
                                       op0=p1, op1=p2), deps)

    def stt(self, out, in0, scalar, in1, op0, op1, deps=(), eng='dve'):
        return self.op(eng, lambda e, o=out, a=in0, s=scalar, c=in1, p0=op0, p1=op1:
                       e.scalar_tensor_tensor(out=o, in0=a, scalar=s, in1=c,
                                              op0=p0, op1=p1), deps)

    def red(self, out, in_, alu, deps=(), eng='dve'):
        return self.op(eng, lambda e, o=out, i=in_, p=alu:
                       e.tensor_reduce(out=o, in_=i, axis=AX.X, op=p), deps)

    def copy(self, out, in_, deps=(), eng='dve'):
        return self.op(eng, lambda e, o=out, i=in_: e.tensor_copy(out=o, in_=i), deps)

    def memset(self, ap, v, deps=(), eng='dve'):
        return self.op(eng, lambda e, o=ap, c=v: e.memset(o, c), deps)

    def recip(self, out, in_, deps=()):
        return self.op('dve', lambda e, o=out, i=in_: e.reciprocal(out=o, in_=i), deps)

    def barrier(self):
        toks = [(s, self.cnt[s]) for s in ('ld', 'wld', 'pe', 'act', 'dve', 'st')
                if self.cnt[s] > 0]
        for s in self.q:
            self.q[s].append(('bar', None, tuple(toks), None))

    def replay(self, stream, eng, sems):
        last = {}
        ownq = {'ld': 'ld', 'wld': 'wld', 'pe': 'pe', 'act': 'act',
                'dve': 'dve', 'st': 'act'}
        pend = []
        for kind, payload, deps, tok in self.q[stream]:
            if kind == 'bar':
                pend.extend(deps)
                continue
            for (ps, val) in list(pend) + list(deps):
                if ps == stream and ps == 'ld':
                    continue
                sem, mult = sems[ps]
                v = val * mult
                if last.get(ps, 0) >= v:
                    continue
                eng.wait_ge(sem, v)
                last[ps] = v
            pend = []
            if kind == 'op':
                inst = payload(eng)
                sem, mult = sems[stream]
                inst.then_inc(sem, mult)
            else:
                dst, src = payload
                sem, mult = sems[tok[0]]
                eng.dma_start(out=dst, in_=src).then_inc(sem, mult)


class Ring:
    def __init__(self, tiles):
        self.tiles = tiles
        self.pend = [[] for _ in tiles]
        self.i = 0

    def get(self):
        k = self.i % len(self.tiles)
        self.i += 1
        d = self.pend[k]
        self.pend[k] = []
        return k, self.tiles[k], d

    def done(self, k, toks):
        self.pend[k].extend(toks)


def build_program():
    nc = bass.Bass("TRN2", target_bir_lowering=False)
    es = ExitStack()

    def din(name, shape, dt=F32):
        return nc.dram_tensor(name, list(shape), dt, kind="ExternalInput").ap()

    def dscr(name, shape, dt):
        return nc.dram_tensor(name, list(shape), dt, kind="Internal").ap()

    x_in = din("x", (SEQ, E))
    w0t = din("w0t", (E, 10272))
    cw = din("cw", (6144, 4))
    cb = din("cb", (6144,))
    dtb0_i = din("dtb0", (32,))
    alog0_i = din("alog0", (32,))
    dd0_i = din("dd0", (32,))
    nw0_i = din("nw0", (4096,))
    wo0t = din("wo0t", (4096, E))
    wg8 = din("wg8", (E, 8192), U8)
    wu8_ = din("wu8", (E, 8192), U8)
    wd8 = din("wd8", (8192, E), U8)
    gsc_i = din("gsc", (8192,))
    usc_i = din("usc", (8192,))
    dsc_i = din("dsc", (E,))
    lnw1_i = din("lnw1", (E,))
    lnb1_i = din("lnb1", (E,))
    win1 = din("win1", (E, 10304))
    dtb1_i = din("dtb1", (64,))
    alog1_i = din("alog1", (64,))
    dd1_i = din("dd1", (64,))
    nw1_i = din("nw1", (4096,))
    wo1 = din("wo1", (4096, E))
    fc1 = din("fc1", (E, 8192))
    fb1_i = din("fb1", (8192,))
    fc2 = din("fc2", (8192, E))
    fb2_i = din("fb2", (E,))
    wr_i = din("wr", (E,))
    br_i = din("br", (1,))
    w1m = din("w1m", (E, 8192))
    b1m_i = din("b1m", (8192,))
    w2m = din("w2m", (8192, E))
    b2m_i = din("b2m", (E,))
    idf_i = din("idf", (128, 128))
    idb_i = din("idb", (128, 128), BF16)
    triu_i = din("triu", (128, 128))
    lowm_i = din("lowm", (128, 128))
    out_d = nc.dram_tensor("out", [SEQ, E], F32, kind="ExternalOutput").ap()

    xT = dscr("xT", (16, 128, SEQ), F32)
    pjh = dscr("pjh", (48, 128, SEQ), BF16)
    gts = dscr("gts", (32, 128, SEQ), BF16)
    cvt = dscr("cvt", (48, 128, SEQ), BF16)
    h3t = dscr("h3t", (32, 128, SEQ), BF16)
    s0t = dscr("s0t", (16, 128, SEQ), BF16)
    ht0 = dscr("ht0", (64, 128, SEQ), BF16)
    x1T = dscr("x1T", (16, 128, SEQ), F32)
    xn1f = dscr("xn1f", (16, 128, SEQ), F32)
    xn1t = dscr("xn1t", (16, 128, SEQ), BF16)
    hs2t = dscr("hs2t", (16, 128, SEQ), BF16)
    zst = dscr("zst", (32, 128, SEQ), BF16)
    pj1 = dscr("pj1", (48, 128, SEQ), BF16)
    ynt = dscr("ynt", (32, 128, SEQ), BF16)
    h1t = dscr("h1t", (64, 128, SEQ), BF16)
    att = dscr("att", (16, 128, SEQ), BF16)
    s1t = dscr("s1t", (16, 128, SEQ), BF16)
    hmt = dscr("hmt", (64, 128, SEQ), BF16)

    def sb(name, shape, dt):
        return es.enter_context(nc.sbuf_tensor(name, list(shape), dt))

    AR = [sb(f"ar{i}", (128, SEQ), BF16) for i in range(16)]
    WB = [sb(f"wb{i}", (128, SEQ), BF16) for i in range(4)]
    WU = [sb(f"wu{i}", (128, SEQ), U8) for i in range(2)]
    EV = [sb(f"ev{i}", (128, 512), F32) for i in range(4)]
    EB = [sb(f"eb{i}", (128, 512), BF16) for i in range(4)]
    ACC = [sb(f"acc{i}", (128, SEQ), F32) for i in range(2)]
    LD2 = [sb(f"ld2{i}", (128, 512), F32) for i in range(2)]
    LB2 = [sb(f"lb2{i}", (128, 512), BF16) for i in range(3)]
    DT_SB = sb("dt_sb", (128, 1024), F32)
    ABT_SB = sb("abt_sb", (128, 1024), F32)
    DTT = ACC[0][0:64, :]
    ABTT = ACC[1][0:64, :]
    SP32 = sb("sp32", (128, 4096), F32)
    SP16 = [sb(f"sp16{i}", (128, 128), BF16) for i in range(2)]
    MASKB = SP32[:, 0:SEQ]
    YG = sb("yg", (128, 8192), BF16)
    CIN = [YG[:, 0:2052], YG[:, 2052:4104]]
    BTb = [sb(f"btb{i}", (128, 256), BF16) for i in range(2)]
    CTb = [sb(f"ctb{i}", (128, 256), BF16) for i in range(2)]
    BSN = [sb(f"bsn{i}", (128, 256), BF16) for i in range(2)]
    GTS = sb("gtsb", (128, 512), F32)
    GSL = [sb(f"gsl{i}", (128, 256), BF16) for i in range(2)]
    XSP = [sb(f"xsp{i}", (128, 256), BF16) for i in range(2)]
    XS = [sb(f"xs{i}", (128, 256), BF16) for i in range(4)]
    CTS = [sb(f"cts{i}", (128, 256), BF16) for i in range(2)]
    MT0 = [sb(f"mt0{i}", (128, 256), BF16) for i in range(2)]
    MT1 = [sb(f"mt1{i}", (128, 128), BF16) for i in range(2)]
    EXF = [sb(f"exf{i}", (128, 128), F32) for i in range(4)]
    SDR = [sb(f"sdr{i}", (128, 128), F32) for i in range(2)]
    YF = sb("yf", (128, 256), F32)
    ACU = [sb(f"acu{i}", (128, 64), F32) for i in range(2)]
    DS = [sb(f"ds{i}", (128, 64), F32) for i in range(2)]
    SD = [sb(f"sd{i}", (128, 64), F32) for i in range(2)]
    DTDS = [sb(f"dtds{i}", (128, 64), F32) for i in range(2)]
    COLF = sb("colf", (128, 64), F32)
    ROWF = sb("rowf", (128, 64), F32)
    ASB = sb("asb", (128, 64), F32)
    AS0B = sb("as0b", (128, 64), F32)
    DAB = sb("dab", (128, 64), F32)
    ROWS = sb("rows", (1, 256), F32)
    DBC0 = sb("dbc0", (128, 32), F32)
    DBC1 = sb("dbc1", (128, 64), F32)
    ACOL0 = sb("acol0", (32, 1), F32)
    ACOL1 = sb("acol1", (64, 1), F32)
    DTBC0 = sb("dtbc0", (32, 1), F32)
    DTBC1 = sb("dtbc1", (64, 1), F32)
    NW0c = sb("nw0c", (128, 32), F32)
    NW1c = sb("nw1c", (128, 32), F32)
    DSCc = sb("dscc", (128, 16), F32)
    GSCc = sb("gscc", (128, 64), F32)
    USCc = sb("uscc", (128, 64), F32)
    FB1c = sb("fb1c", (128, 64), F32)
    FB2c = sb("fb2c", (128, 16), F32)
    B1Mc = sb("b1mc", (128, 64), F32)
    B2Mc = sb("b2mc", (128, 16), F32)
    LNWc = sb("lnwc", (128, 16), F32)
    LNBc = sb("lnbc", (128, 16), F32)
    WRC = sb("wrc", (128, 16), F32)
    WRB = sb("wrb", (128, 16), BF16)
    CWc = [sb(f"cwc{i}", (128, 4), F32) for i in range(2)]
    CBc = [sb(f"cbc{i}", (128, 1), F32) for i in range(2)]
    SROW = sb("srow", (1, SEQ), F32)
    CROW = sb("crow", (1, SEQ), F32)
    BRC = sb("brc", (1, 1), F32)
    SC = [sb(f"sc{i}", (1, 1), F32) for i in range(7)]
    ONESC = sb("onesc", (128, 1), F32)
    ONESR = sb("onesr", (1, 128), F32)
    ONE128 = sb("one128", (128, 128), F32)
    IDF = sb("idfs", (128, 128), F32)
    IDB = sb("idbs", (128, 128), BF16)
    TRIU = sb("trius", (128, 128), F32)
    LOWM = sb("lowms", (128, 128), F32)
    EPSC = sb("epsc", (128, 4), F32)
    MRW = sb("mrw", (1, 512), F32)
    MRW2 = sb("mrw2", (1, 512), F32)
    MBC = ACC[1][:, 0:512]
    RBC = ACC[1][:, 512:1024]
    LNT = [ACC[0][:, 0:512], ACC[0][:, 512:1024]]

    def psa(name, shape, dt):
        return es.enter_context(nc.psum_tensor(name, list(shape), dt))

    PA = psa("pa", (128, 512), F32)
    PB = psa("pb", (128, 512), F32)
    PY = psa("py", (128, 256), F32)
    PS2 = psa("ps2", (128, 512), F32)
    PT0 = psa("pt0", (128, 128), F32)
    PT1 = psa("pt1", (128, 128), F32)
    PTB0 = psa("ptb0", (128, 128), BF16)
    PTB1 = psa("ptb1", (128, 128), BF16)

    sem_objs = {s: es.enter_context(nc.semaphore(s + "_sem"))
                for s in ('ld', 'wld', 'pe', 'act', 'dve', 'st')}
    sems = {s: (sem_objs[s], 16 if s in ('ld', 'st') else 1)
            for s in sem_objs}

    b = Bld()
    rEB = Ring(EB)
    rEV = Ring(EV)
    nc.const_aps.register(F32, 1e-5, EPSC[:, 0:1])
    nc.const_aps.register(F32, 1e-6, EPSC[:, 1:2])

    b.memset(EPSC[:, 0:1], 1e-5)
    b.memset(EPSC[:, 1:2], 1e-6)
    b.load(IDF[:], idf_i)
    b.load(IDB[:], idb_i)
    b.load(TRIU[:], triu_i)
    b.load(LOWM[:], lowm_i)
    b.memset(ONESC[:], 1.0)
    b.memset(ONESR[:], 1.0)
    b.memset(ONE128[:], 1.0)
    b.memset(CIN[0][:, 0:4], 0.0)
    b.memset(CIN[1][:, 0:4], 0.0)
    t1 = b.load(ACOL0[:], alog0_i[:, None])
    t2 = b.load(ACOL1[:], alog1_i[:, None])
    b.load(DTBC0[:], dtb0_i[:, None])
    b.load(DTBC1[:], dtb1_i[:, None])
    e1 = b.act(ACOL0[:], ACOL0[:], AF.Exp, deps=[t1])
    b.ts(ACOL0[:], ACOL0[:], -1.0, ALU.mult, deps=[e1])
    e2 = b.act(ACOL1[:], ACOL1[:], AF.Exp, deps=[t2])
    b.ts(ACOL1[:], ACOL1[:], -1.0, ALU.mult, deps=[e2])
    tD0 = b.load(ROWS[0:1, 0:32], dd0_i[None, :])
    m1 = b.mm(PT0[:, 0:32], ONESR[:], ROWS[0:1, 0:32], True, True, deps=[tD0])
    cD0 = b.act(DBC0[:], PT0[:, 0:32], AF.Copy, deps=[m1])
    tD1 = b.load(ROWS[0:1, 0:64], dd1_i[None, :], deps=[m1])
    m2 = b.mm(PT1[:, 0:64], ONESR[:], ROWS[0:1, 0:64], True, True, deps=[tD1])
    b.act(DBC1[:], PT1[:, 0:64], AF.Copy, deps=[m2])
    tbr = b.load(BRC[:], br_i[None, :])
    for j in range(32):
        b.load(NW0c[:, j:j + 1], nw0_i[j * 128:(j + 1) * 128][:, None])
        b.load(NW1c[:, j:j + 1], nw1_i[j * 128:(j + 1) * 128][:, None])
    wrt = []
    for j in range(16):
        b.load(DSCc[:, j:j + 1], dsc_i[j * 128:(j + 1) * 128][:, None])
        b.load(FB2c[:, j:j + 1], fb2_i[j * 128:(j + 1) * 128][:, None])
        b.load(B2Mc[:, j:j + 1], b2m_i[j * 128:(j + 1) * 128][:, None])
        b.load(LNWc[:, j:j + 1], lnw1_i[j * 128:(j + 1) * 128][:, None])
        b.load(LNBc[:, j:j + 1], lnb1_i[j * 128:(j + 1) * 128][:, None])
        wrt.append(b.load(WRC[:, j:j + 1], wr_i[j * 128:(j + 1) * 128][:, None]))
    for j in range(64):
        b.load(GSCc[:, j:j + 1], gsc_i[j * 128:(j + 1) * 128][:, None])
        b.load(USCc[:, j:j + 1], usc_i[j * 128:(j + 1) * 128][:, None])
        b.load(FB1c[:, j:j + 1], fb1_i[j * 128:(j + 1) * 128][:, None])
        b.load(B1Mc[:, j:j + 1], b1m_i[j * 128:(j + 1) * 128][:, None])
    b.act(WRB[:], WRC[:], AF.Copy, deps=wrt)
    b.barrier()

    accw = [[], []]
    ebw = {}
    ptbw = [[], []]
    ptw = [[], []]
    evw3 = [[], []]
    lnscr = [[], []]
    for t in range(NT):
        xa = ACC[t % 2]
        tl = b.load(xa[:], x_in[t * 128:(t + 1) * 128, :], deps=accw[t % 2])
        accw[t % 2] = []
        s1 = b.red(EV[0][:, 0:1], xa[:], ALU.add, deps=[tl])
        q1 = b.act(EV[1][:], xa[:, 0:512], AF.Square, deps=[tl] + lnscr[0])
        q2 = b.act(EV[2][:], xa[:, 512:1024], AF.Square, deps=[tl] + lnscr[1])
        r1 = b.red(EV[0][:, 1:2], EV[1][:], ALU.add, deps=[q1])
        r2 = b.red(EV[0][:, 2:3], EV[2][:], ALU.add, deps=[q2])
        q3 = b.act(EV[1][:], xa[:, 1024:1536], AF.Square, deps=[r1])
        q4 = b.act(EV[2][:], xa[:, 1536:2048], AF.Square, deps=[r2])
        r3 = b.red(EV[0][:, 6:7], EV[1][:], ALU.add, deps=[q3, r1])
        r4 = b.red(EV[0][:, 7:8], EV[2][:], ALU.add, deps=[q4, r2])
        lnscr[0] = [r3]
        lnscr[1] = [r4]
        a1 = b.tt(EV[0][:, 1:2], EV[0][:, 1:2], EV[0][:, 2:3], ALU.add, deps=[r1, r2])
        a2 = b.tt(EV[0][:, 6:7], EV[0][:, 6:7], EV[0][:, 7:8], ALU.add, deps=[r3, r4])
        a3 = b.tt(EV[0][:, 3:4], EV[0][:, 1:2], EV[0][:, 6:7], ALU.add, deps=[a1, a2])
        mm_ = b.ts(EV[0][:, 0:1], EV[0][:, 0:1], 1.0 / 2048, ALU.mult, deps=[s1])
        msq = b.tt(EV[0][:, 4:5], EV[0][:, 0:1], EV[0][:, 0:1], ALU.mult, deps=[mm_])
        var = b.ts2(EV[0][:, 3:4], EV[0][:, 3:4], 1.0 / 2048, ALU.mult, 0.0, ALU.add,
                    deps=[a3])
        var2 = b.tt(EV[0][:, 3:4], EV[0][:, 3:4], EV[0][:, 4:5], ALU.subtract,
                    deps=[var, msq])
        sd_ = b.act(EV[0][:, 5:6], EV[0][:, 3:4], AF.Sqrt, deps=[var2], bias=1e-5)
        rs_ = b.recip(EV[0][:, 5:6], EV[0][:, 5:6], deps=[sd_])
        for ci in range(4):
            xn = b.ts2(EB[ci][:], xa[:, ci * 512:(ci + 1) * 512], EV[0][:, 0:1],
                       ALU.subtract, EV[0][:, 5:6], ALU.mult,
                       deps=[rs_, mm_, tl] + ebw.get(ci, []))
            trs = []
            for j in range(4):
                kk = ci * 4 + j
                pi = kk % 2
                ptb = [PTB0, PTB1][pi]
                tr = b.trans(ptb[:], EB[ci][:, j * 128:(j + 1) * 128], IDB[:],
                             deps=[xn] + ptbw[pi])
                cp = b.copy(AR[kk][:, t * 128:(t + 1) * 128], ptb[:], deps=[tr])
                ptbw[pi] = [cp]
                trs.append(tr)
            ebw[ci] = trs
        for kk in range(16):
            pi = kk % 2
            pt = [PT0, PT1][pi]
            tr = b.trans(pt[:], xa[:, kk * 128:(kk + 1) * 128], IDF[:],
                         deps=[tl] + ptw[pi])
            half = EV[3][:, pi * 128:(pi + 1) * 128]
            cp = b.act(half, pt[:], AF.Copy, deps=[tr] + evw3[pi])
            st_ = b.store(xT[kk, :, t * 128:(t + 1) * 128], half, deps=[cp])
            ptw[pi] = [cp]
            evw3[pi] = [st_]
            accw[t % 2].append(tr)
        accw[t % 2] += [q4, r4, xn]
    b.barrier()

    def wload(Wsrc, k0glob, Kt, c0, cM, pair_i, war):
        stg = WB[2 * pair_i].bitcast(F32)
        out = WB[2 * pair_i + 1]
        nper = max(1, 1024 // cM)
        ctoks = [None] * Kt
        prev_cast = []
        for hf in range((Kt + nper - 1) // nper):
            k0 = hf * nper
            kn = min(nper, Kt - k0)
            lts = []
            for k in range(kn):
                dep = war + prev_cast if k == 0 else []
                lts.append(b.load(stg[:, k * cM:(k + 1) * cM],
                                  Wsrc[(k0glob + k0 + k) * 128:
                                       (k0glob + k0 + k + 1) * 128, c0:c0 + cM],
                                  deps=dep))
            ct = b.op('wld', lambda e, o=out[:, k0 * cM:(k0 + kn) * cM],
                      i=stg[:, 0:kn * cM]: e.tensor_copy(out=o, in_=i),
                      deps=lts + war)
            prev_cast = [ct]
            for k in range(kn):
                ctoks[k0 + k] = ct
        return out, ctoks, prev_cast

    def gemm_c(Wsrc, ctiles, cM, evac, Kt=16):
        pwar = [[], []]
        pswar = {0: [], 1: []}
        for ci in range(len(ctiles)):
            c0 = ctiles[ci]
            pi = ci % 2
            wb, ctoks, casts = wload(Wsrc, 0, Kt, c0, cM, pi, pwar[pi])
            pwar[pi] = list(casts)
            t0 = 0
            for ti in range(4):
                tw = 512
                pp = [PA, PB][ti % 2]
                mmt = None
                for k in range(Kt):
                    mmt = b.mm(pp[0:cM, 0:tw], wb[:, k * cM:(k + 1) * cM],
                               AR[k][:, t0:t0 + tw], k == 0, k == Kt - 1,
                               deps=[ctoks[k]] + (pswar[ti % 2] if k == 0 else []))
                    if k == 0:
                        pswar[ti % 2] = []
                    pwar[pi].append(mmt)
                pswar[ti % 2] = list(evac(ci, ti, t0, tw, pp[0:cM, 0:tw], mmt))
                t0 += tw

    def mk_store_evac(dst_slabs, func):
        def evac(ci, ti, t0, tw, pp, mmt):
            k, ebuf, war = rEB.get()
            cp = b.act(ebuf[:, 0:tw], pp, func, deps=[mmt] + war)
            st_ = b.store(dst_slabs[ci][:, t0:t0 + tw], ebuf[:, 0:tw], deps=[cp])
            rEB.done(k, [st_])
            return [cp]
        return evac

    gemm_c(w0t, [4096 + 128 * i for i in range(48)], 128,
           mk_store_evac([pjh[i] for i in range(48)], AF.Copy))
    b.barrier()
    gemm_c(w0t, [128 * i for i in range(32)], 128,
           mk_store_evac([gts[i] for i in range(32)], AF.Silu))
    b.barrier()

    def dt_stage(Wsrc, col0, H, dtbc, acol):
        state = {'d': []}

        def ev_dt(ci, ti, t0, tw, pp, mmt):
            r_ = b.act(DTT[0:H, t0:t0 + tw], pp, AF.Relu, deps=[mmt], bias=dtbc[0:H, :])
            a_ = b.act(EV[0][0:H, 0:tw], pp, AF.Abs, deps=[mmt] + state['d'],
                       bias=dtbc[0:H, :])
            e_ = b.act(EV[1][0:H, 0:tw], EV[0][0:H, 0:tw], AF.Exp, deps=[a_], scale=-1.0)
            l_ = b.act(EV[0][0:H, 0:tw], EV[1][0:H, 0:tw], AF.Ln, deps=[e_], bias=1.0)
            d_ = b.tt(DTT[0:H, t0:t0 + tw], DTT[0:H, t0:t0 + tw], EV[0][0:H, 0:tw],
                      ALU.add, deps=[l_, r_])
            ab = b.ts(ABTT[0:H, t0:t0 + tw], DTT[0:H, t0:t0 + tw], acol[0:H, :],
                      ALU.mult, deps=[d_])
            state['d'] = [d_]
            return [a_, d_, ab]
        gemm_c(Wsrc, [col0], H, ev_dt)
        dtw = [[], []]
        for t in range(NT):
            pt = [PT0, PT1][t % 2]
            tr = b.trans(pt[0:128, 0:H], DTT[0:H, t * 128:(t + 1) * 128],
                         IDF[0:H, 0:H], deps=dtw[t % 2])
            c1 = b.act(DT_SB[:, t * H:(t + 1) * H], pt[0:128, 0:H], AF.Copy, deps=[tr])
            tr2 = b.trans(pt[0:128, 0:H], ABTT[0:H, t * 128:(t + 1) * 128],
                          IDF[0:H, 0:H], deps=[c1])
            c2 = b.act(ABT_SB[:, t * H:(t + 1) * H], pt[0:128, 0:H], AF.Copy, deps=[tr2])
            dtw[t % 2] = [c2]
    dt_stage(w0t, 10240, 32, DTBC0, ACOL0)
    b.barrier()

    cinw = [[], []]
    cwW = [[], []]
    convw = []
    for c in range(48):
        cinb = CIN[c % 2]
        cwb = CWc[c % 2]
        cbb = CBc[c % 2]
        tl = b.load(cinb[:, 4:4 + SEQ], pjh[c], deps=cinw[c % 2])
        tw_ = b.load(cwb[:], cw[c * 128:(c + 1) * 128, :], deps=cwW[c % 2])
        tb_ = b.load(cbb[:], cb[c * 128:(c + 1) * 128][:, None], deps=cwW[c % 2])
        t0_ = b.ts(ACC[0][:], cinb[:, 1:1 + SEQ], cwb[:, 0:1], ALU.mult,
                   deps=[tl, tw_] + convw)
        t1_ = b.stt(ACC[1][:], cinb[:, 2:2 + SEQ], cwb[:, 1:2], ACC[0][:],
                    ALU.mult, ALU.add, deps=[t0_], eng='gps' if False else 'dve')
        t2_ = b.stt(ACC[0][:], cinb[:, 3:3 + SEQ], cwb[:, 2:3], ACC[1][:],
                    ALU.mult, ALU.add, deps=[t1_])
        t3_ = b.stt(ACC[1][:], cinb[:, 4:4 + SEQ], cwb[:, 3:4], ACC[0][:],
                    ALU.mult, ALU.add, deps=[t2_, tb_])
        convw = []
        sts = []
        for q4 in range(4):
            k, ebuf, war = rEB.get()
            s_ = b.act(ebuf[:, 0:512], ACC[1][:, q4 * 512:(q4 + 1) * 512], AF.Silu,
                       deps=[t3_] + war, bias=cbb[:])
            st_ = b.store(cvt[c, :, q4 * 512:(q4 + 1) * 512], ebuf[:, 0:512], deps=[s_])
            rEB.done(k, [st_])
            convw.append(s_)
            sts.append(st_)
        cinw[c % 2] = [t3_]
        cwW[c % 2] = [t3_] + convw
    b.barrier()

    def ssd(cfg):
        H, P, N = cfg['H'], cfg['P'], cfg['N']
        CH = 256
        ngrp, hpg = cfg['ngrp'], cfg['hpg']
        sp32 = SP32[0:N, :]
        b.memset(SP32[:], 0.0)
        st = {
            'bwar': [[], []], 'cwar': [[], []], 'xwar': [[], []],
            'gtwar': [], 'exwar': [[], []], 'ptw': [[], []], 'ptbw': [[], []],
            'e2war': [], 'e3war': [], 'mt0war': [], 'mt1war': [],
            'xs0war': [], 'xs1war': [], 'xs2war': [[], []],
            'ctswar': [[], []], 'spwar': [[], []],
            'pywar': [], 'stwar': [], 'gslwar': [[], []],
            'sdrwar': [[], []], 'lastx': [None, None], 'lastz': None,
            'ygtoks': [], 'rmswar': [], 'mbcw': [], 'lastyg': [],
        }
        for c in range(8):
            t0i, t1i = 2 * c, 2 * c + 1
            ab0 = ABT_SB[:, t0i * H:(t0i + 1) * H]
            ab1 = ABT_SB[:, t1i * H:(t1i + 1) * H]
            dt0 = DT_SB[:, t0i * H:(t0i + 1) * H]
            dt1 = DT_SB[:, t1i * H:(t1i + 1) * H]
            m1_ = b.mm(PS2[0:128, 0:H], TRIU[:], ab0, True, True)
            c1_ = b.act(ACU[0][:, 0:H], PS2[0:128, 0:H], AF.Copy, deps=[m1_])
            m2_ = b.mm(PS2[0:128, 0:H], ONE128[:], ab0, True, False, deps=[c1_])
            m2b = b.mm(PS2[0:128, 0:H], TRIU[:], ab1, False, True)
            c2_ = b.act(ACU[1][:, 0:H], PS2[0:128, 0:H], AF.Copy, deps=[m2b])
            r1_ = b.mm(PS2[0:1, 128:128 + H], ONESC[:], ab0, True, True, deps=[c2_])
            cr1 = b.act(ROWS[0:1, 0:H], PS2[0:1, 128:128 + H], AF.Copy, deps=[r1_])
            r2_ = b.mm(PS2[0:1, 128:128 + H], ONESC[:], ab1, True, True, deps=[cr1])
            cr2 = b.act(ROWS[0:1, 64:64 + H], PS2[0:1, 128:128 + H], AF.Copy, deps=[r2_])
            su = b.tt(ROWS[0:1, 128:128 + H], ROWS[0:1, 0:H], ROWS[0:1, 64:64 + H],
                      ALU.add, deps=[cr1, cr2])
            bc1 = b.mm(PS2[0:128, 128:128 + H], ONESR[:], ROWS[0:1, 128:128 + H],
                       True, True, deps=[su, cr2])
            cb1 = b.act(ASB[:, 0:H], PS2[0:128, 128:128 + H], AF.Copy, deps=[bc1])
            da = b.act(DAB[:, 0:H], ASB[:, 0:H], AF.Exp, deps=[cb1])
            bc2 = b.mm(PS2[0:128, 128:128 + H], ONESR[:], ROWS[0:1, 0:H],
                       True, True, deps=[cb1])
            cb2 = b.act(AS0B[:, 0:H], PS2[0:128, 128:128 + H], AF.Copy, deps=[bc2])
            dsx = []
            for i in range(2):
                ebuf = EV[i]
                q1 = b.tt(ebuf[:, 0:H], ASB[:, 0:H], [ACU[0], ACU[1]][i][:, 0:H],
                          ALU.subtract, deps=[cb1, c1_, c2_])
                q2 = b.act(DS[i][:, 0:H], ebuf[0:128, 0:H], AF.Exp, deps=[q1])
                q3 = b.act(SD[i][:, 0:H], [ACU[0], ACU[1]][i][:, 0:H], AF.Exp,
                           deps=[c1_, c2_])
                q4 = b.tt(DTDS[i][:, 0:H], DS[i][:, 0:H], [dt0, dt1][i], ALU.mult,
                          deps=[q2])
                dsx += [q2, q3, q4]
            f1_ = b.tt(EV[2][:, 0:H], AS0B[:, 0:H], ACU[0][:, 0:H], ALU.subtract,
                       deps=[cb2, c1_])
            f2_ = b.act(COLF[:, 0:H], EV[2][0:128, 0:H], AF.Exp, deps=[f1_])
            f3_ = b.tt(EV[3][:, 0:H], ACU[1][:, 0:H], AS0B[:, 0:H], ALU.subtract,
                       deps=[cb2, c2_])
            f4_ = b.act(ROWF[:, 0:H], EV[3][0:128, 0:H], AF.Exp, deps=[f3_])
            for g in range(ngrp):
                gi = g % 2
                tB = b.load(BTb[gi][0:N, 0:256],
                            cfg['bsrc'](g)[:, c * CH:(c + 1) * CH], deps=st['bwar'][gi])
                st['bwar'][gi] = []
                tX = b.load(XSP[gi][0:P, 0:256],
                            cfg['xsrc'](g)[:, c * CH:(c + 1) * CH], deps=st['xwar'][gi])
                st['xwar'][gi] = []
                st['lastx'][gi] = tX
                bsn_t = []
                xsp_t = []
                for i in range(2):
                    ptb = [PTB0, PTB1][i]
                    tr = b.trans(ptb[0:128, 0:N], BTb[gi][0:N, i * 128:(i + 1) * 128],
                                 IDB[0:N, 0:N], deps=[tB] + st['ptbw'][i])
                    cp = b.copy(BSN[gi][:, i * N:(i + 1) * N], ptb[0:128, 0:N],
                                deps=[tr])
                    bsn_t.append(cp)
                    tr2 = b.trans(ptb[0:128, 0:P], XSP[gi][0:P, i * 128:(i + 1) * 128],
                                  IDB[0:P, 0:P], deps=[tX, cp])
                    cp2 = b.copy(XS[2 + gi][:, i * P:(i + 1) * P], ptb[0:128, 0:P],
                                 deps=[tr2] + st['xs2war'][gi])
                    xsp_t.append(cp2)
                    st['ptbw'][i] = [cp2]
                    st['bwar'][gi].append(tr)
                    st['xwar'][gi].append(tr2)
                st['xs2war'][gi] = []
                for hj in range(hpg):
                    h = cfg['head'](g, hj)
                    hi = h % 2
                    tC = b.load(CTb[hi][0:N, 0:256],
                                cfg['csrc'](h)[:, c * CH:(c + 1) * CH],
                                deps=st['cwar'][hi])
                    st['cwar'][hi] = []
                    gts_t = []
                    for i in range(2):
                        mg = b.mm(PA[0:128, i * 256:(i + 1) * 256],
                                  BTb[gi][0:N, i * 128:(i + 1) * 128],
                                  CTb[hi][0:N, 0:256], True, True,
                                  deps=[tB, tC] + st['gtwar'])
                        cg = b.act(GTS[:, i * 256:(i + 1) * 256],
                                   PA[0:128, i * 256:(i + 1) * 256], AF.Copy,
                                   deps=[mg])
                        gts_t.append(cg)
                    st['gtwar'] = gts_t
                    st['bwar'][gi] += [gts_t[0], gts_t[1]]
                    st['cwar'][hi] += gts_t
                    rb = []
                    for i in range(2):
                        cbx = b.act(EXF[i][:], ONE128[:], AF.Copy,
                                    scale=[ACU[0], ACU[1]][i][:, h:h + 1],
                                    deps=[c1_, c2_] + st['exwar'][i])
                        trx = b.trans([PT0, PT1][i][:], EXF[i][:], IDF[:],
                                      deps=[cbx] + st['ptw'][i])
                        rb.append(trx)
                        st['exwar'][i] = [trx]
                    sdr_t = []
                    for i in range(2):
                        sx = b.act(SDR[i][:], [PT0, PT1][i][:], AF.Exp,
                                   deps=[rb[i]] + st['sdrwar'][i])
                        sdr_t.append(sx)
                    d00 = b.ts(EXF[2][:], PT0[:], ACU[0][:, h:h + 1], ALU.subtract,
                               deps=[rb[0], sdr_t[0]] + st['e2war'])
                    d00b = b.tt(EXF[2][:], EXF[2][:], TRIU[:], ALU.mult, deps=[d00])
                    d00c = b.tt(EXF[2][:], EXF[2][:], LOWM[:], ALU.add, deps=[d00b])
                    e00 = b.act(EXF[2][:], EXF[2][:], AF.Exp, deps=[d00c])
                    mt00 = b.tt(MT0[c % 2][:, 0:128], EXF[2][:], GTS[:, 0:128],
                                ALU.mult, deps=[e00, gts_t[0]] + st['mt0war'])
                    cbro = b.act(EXF[3][:], ONE128[:], AF.Copy,
                                 scale=ROWF[:, h:h + 1], deps=[f4_] + st['e3war'])
                    tro = b.trans(PY[0:128, 0:128], EXF[3][:], IDF[:],
                                  deps=[cbro] + st['pywar'])
                    d01 = b.ts(EXF[3][:], PY[0:128, 0:128], COLF[:, h:h + 1],
                               ALU.mult, deps=[tro, f2_])
                    mt01 = b.tt(MT0[c % 2][:, 128:256], EXF[3][:], GTS[:, 128:256],
                                ALU.mult, deps=[d01, gts_t[0]])
                    st['e3war'] = [mt01]
                    d11 = b.ts(EXF[2][:], PT1[:], ACU[1][:, h:h + 1], ALU.subtract,
                               deps=[rb[1], sdr_t[1], mt00])
                    d11b = b.tt(EXF[2][:], EXF[2][:], TRIU[:], ALU.mult, deps=[d11])
                    d11c = b.tt(EXF[2][:], EXF[2][:], LOWM[:], ALU.add, deps=[d11b])
                    e11 = b.act(EXF[2][:], EXF[2][:], AF.Exp, deps=[d11c])
                    mt11 = b.tt(MT1[c % 2][:], EXF[2][:], GTS[:, 384:512],
                                ALU.mult, deps=[e11, gts_t[1]] + st['mt1war'])
                    st['e2war'] = [mt11]
                    st['ptw'][0] = [d00]
                    st['ptw'][1] = [d11]
                    xt = []
                    for i in range(2):
                        x1_ = b.ts(XS[0][:, i * P:(i + 1) * P],
                                   XS[2 + gi][:, i * P:(i + 1) * P],
                                   DT_SB[:, (2 * c + i) * H + h:(2 * c + i) * H + h + 1],
                                   ALU.mult, deps=xsp_t + st['xs0war'])
                        x2_ = b.ts(XS[1][:, i * P:(i + 1) * P],
                                   XS[2 + gi][:, i * P:(i + 1) * P],
                                   DTDS[i][:, h:h + 1],
                                   ALU.mult, deps=xsp_t + dsx + st['xs1war'])
                        xt += [x1_, x2_]
                        st['xs2war'][gi] += [x1_, x2_]
                    st['xs0war'] = []
                    st['xs1war'] = []
                    ct_t = []
                    for i in range(2):
                        cc = b.tt(CTS[hi][0:N, i * 128:(i + 1) * 128],
                                  CTb[hi][0:N, i * 128:(i + 1) * 128],
                                  SDR[i][0:N, :], ALU.mult,
                                  deps=[tC, sdr_t[i]] + st['ctswar'][hi])
                        ct_t.append(cc)
                        st['sdrwar'][i] = [cc]
                    st['ctswar'][hi] = []
                    st['cwar'][hi] += ct_t
                    spc = b.act(SP16[hi][0:N, 0:P], sp32[:, h * P:(h + 1) * P],
                                AF.Copy, deps=st['spwar'][hi])
                    y1 = b.mm(PY[0:P, 0:128], XS[0][:, 0:P], MT0[c % 2][:, 0:128],
                              True, False, deps=[xt[0], mt00, d01] + st['pywar'])
                    st['pywar'] = []
                    y2 = b.mm(PY[0:P, 0:128], SP16[hi][0:N, 0:P], CTS[hi][0:N, 0:128],
                              False, True, deps=[spc, ct_t[0]])
                    y3 = b.mm(PY[0:P, 128:256], XS[0][:, 0:P], MT0[c % 2][:, 128:256],
                              True, False, deps=[xt[0], mt01])
                    y4 = b.mm(PY[0:P, 128:256], XS[0][:, P:2 * P], MT1[c % 2][:],
                              False, False, deps=[xt[2], mt11])
                    y5 = b.mm(PY[0:P, 128:256], SP16[hi][0:N, 0:P],
                              CTS[hi][0:N, 128:256], False, True, deps=[spc, ct_t[1]])
                    st['mt0war'] = [y1, y3]
                    st['mt1war'] = [y4]
                    st['ctswar'][hi] += [y2, y5]
                    st['spwar'][hi] = [y2, y5]
                    s1_ = b.mm(PS2[0:N, 256:256 + P], BSN[gi][:, 0:N], XS[1][:, 0:P],
                               True, False,
                               deps=[bsn_t[0], xt[1]] + st['stwar'] + st['rmswar'])
                    s2_ = b.mm(PS2[0:N, 256:256 + P], BSN[gi][:, N:2 * N],
                               XS[1][:, P:2 * P], False, True,
                               deps=[bsn_t[1], xt[3]])
                    st['stwar'] = []
                    st['bwar'][gi] += [s1_, s2_]
                    st['xs1war'] += [s1_, s2_]
                    st['xs0war'] += [y1, y3, y4]
                    gtile, gdep = cfg['zload'](st, c, h)
                    yv = b.stt(YF[0:P, :], XSP[gi][0:P, 0:256],
                               cfg['dbc'][0:P, h:h + 1], PY[0:P, 0:256],
                               ALU.mult, ALU.add,
                               deps=[y2, y5, st['lastx'][gi], cD0] + st['lastyg'])
                    st['pywar'] = [yv]
                    st['xwar'][gi].append(yv)
                    yg_ = b.tt(cfg['ygdst'](h), YF[0:cfg['ygp'], :], gtile,
                               ALU.mult, deps=[yv, gdep])
                    st['lastyg'] = [yg_]
                    st['ygtoks'].append(yg_)
                    cfg['zwar'](st, c, h, [yg_])
                    up = b.stt(sp32[:, h * P:(h + 1) * P], sp32[:, h * P:(h + 1) * P],
                               DAB[0:N, h:h + 1], PS2[0:N, 256:256 + P],
                               ALU.mult, ALU.add, deps=[s2_, da, spc])
                    st['spwar'][hi] = st['spwar'][hi] + [up]
                    st['stwar'] = [up]
            cfg['rms'](st, c)
            b.barrier()

    rmsev = {}

    def rms_generic(st, c, nslab, denom, eps, nwc, dst, grp):
        for (s0, cnt) in grp:
            mmts = []
            for j in range(cnt):
                sl = YG[:, (s0 + j) * 256:(s0 + j + 1) * 256]
                ei = j % 2
                q = b.act(EV[ei][:, 0:256], sl, AF.Square,
                          deps=st['ygtoks'] + rmsev.get(ei, []))
                mt_ = b.mm(PS2[0:1, 0:256], ONESC[:], EV[ei][:, 0:256],
                           j == 0, j == cnt - 1,
                           deps=[q] + (st['rmswar'] if j == 0 else []))
                rmsev[ei] = [mt_]
                mmts.append(mt_)
            st['rmswar'] = []
            sq = b.act(MRW[0:1, 0:256], PS2[0:1, 0:256], AF.Sqrt, deps=[mmts[-1]],
                       scale=1.0 / denom, bias=eps)
            rc = b.recip(MRW[0:1, 0:256], MRW[0:1, 0:256], deps=[sq])
            bcm = b.mm(PS2[0:128, 256:512], ONESR[:], MRW[0:1, 0:256], True, True,
                       deps=[rc] + st['stwar'] + st['mbcw'])
            bce = b.act(MBC[:, 0:256], PS2[0:128, 256:512], AF.Copy, deps=[bcm])
            st['rmswar'] = [bce, sq]
            for j in range(cnt):
                sl = YG[:, (s0 + j) * 256:(s0 + j + 1) * 256]
                ei = j % 2
                t1_ = b.tt(EV[ei][:, 0:256], sl, MBC[:, 0:256], ALU.mult,
                           deps=[bce] + rmsev.get(ei, []))
                k, ebuf, war = rEB.get()
                t2_ = b.ts(ebuf[:, 0:256], EV[ei][:, 0:256], nwc[:, s0 + j:s0 + j + 1],
                           ALU.mult, deps=[t1_] + war)
                st_ = b.store(dst[s0 + j][:, c * 256:(c + 1) * 256], ebuf[:, 0:256],
                              deps=[t2_])
                rEB.done(k, [st_])
                rmsev[ei] = [t2_]
                st['mbcw'] = [t1_]
        st['ygtoks'] = []

    def l0_zload(st, c, h):
        hi = h % 2
        tok = b.load(GSL[hi][:, 0:256], gts[h][:, c * 256:(c + 1) * 256],
                     deps=st['gslwar'][hi])
        st['gslwar'][hi] = []
        return GSL[hi][:, 0:256], tok

    def l0_zwar(st, c, h, toks):
        st['gslwar'][h % 2] += toks

    l0_cfg = dict(
        H=32, P=128, N=128, ngrp=32, hpg=1,
        head=lambda g, j: g,
        bsrc=lambda g: cvt[32 + g % 8],
        csrc=lambda h: cvt[40 + h % 8],
        xsrc=lambda g: cvt[g],
        dbc=DBC0, ygp=128,
        ygdst=lambda h: YG[:, h * 256:(h + 1) * 256],
        zload=l0_zload, zwar=l0_zwar,
        rms=lambda st, c: rms_generic(st, c, 32, 512.0, 1e-6, NW0c, h3t,
                                      [(4 * r, 4) for r in range(8)]),
    )
    ssd(l0_cfg)
    b.barrier()

    def f1_stage():
        u8war = [[], []]
        wwar = [[], [], [], []]
        pswar = {0: [], 1: []}
        for it in range(64):
            wu = WU[it % 2]
            wbg = WB[(it % 2) * 2]
            wbu = WB[(it % 2) * 2 + 1]
            lts = []
            for k in range(KT):
                lt = b.load(wu[:, k * 128:(k + 1) * 128],
                            wg8[k * 128:(k + 1) * 128, it * 128:(it + 1) * 128],
                            deps=u8war[it % 2] if k == 0 else [])
                lts.append(lt)
            dq1 = b.act(wbg[:], wu[:], AF.Copy,
                        deps=lts + wwar[(it % 2) * 2], bias=-128.0)
            lts2 = []
            for k in range(KT):
                lt = b.load(wu[:, k * 128:(k + 1) * 128],
                            wu8_[k * 128:(k + 1) * 128, it * 128:(it + 1) * 128],
                            deps=[dq1] if k == 0 else [])
                lts2.append(lt)
            dq2 = b.act(wbu[:], wu[:], AF.Copy,
                        deps=lts2 + wwar[(it % 2) * 2 + 1], bias=-128.0)
            u8war[it % 2] = [dq2]
            wwar[(it % 2) * 2] = []
            wwar[(it % 2) * 2 + 1] = []
            for ti in range(4):
                t0 = ti * 512
                mg = mu = None
                for k in range(KT):
                    mg = b.mm(PA[0:128, 0:512], wbg[:, k * 128:(k + 1) * 128],
                              AR[k][:, t0:t0 + 512], k == 0, k == KT - 1,
                              deps=[dq1] + (pswar[0] if k == 0 else []))
                    if k == 0:
                        pswar[0] = []
                    wwar[(it % 2) * 2].append(mg)
                for k in range(KT):
                    mu = b.mm(PB[0:128, 0:512], wbu[:, k * 128:(k + 1) * 128],
                              AR[k][:, t0:t0 + 512], k == 0, k == KT - 1,
                              deps=[dq2] + (pswar[1] if k == 0 else []))
                    if k == 0:
                        pswar[1] = []
                    wwar[(it % 2) * 2 + 1].append(mu)
                k1, ev1, w1 = rEV.get()
                g_ = b.act(ev1[:], PA[0:128, 0:512], AF.Silu, deps=[mg] + w1,
                           scale=GSCc[:, it:it + 1])
                k2, ev2_, w2 = rEV.get()
                u_ = b.ts(ev2_[:], PB[0:128, 0:512], USCc[:, it:it + 1],
                          ALU.mult, deps=[mu] + w2)
                k3, ebuf, w3 = rEB.get()
                h_ = b.tt(ebuf[:], ev1[:], ev2_[:], ALU.mult, deps=[g_, u_] + w3)
                st_ = b.store(ht0[it, :, t0:t0 + 512], ebuf[:], deps=[h_])
                rEV.done(k1, [h_])
                rEV.done(k2, [h_])
                rEB.done(k3, [st_])
                pswar[0] = [g_]
                pswar[1] = [u_]
    f1_stage()
    b.barrier()

    def gemm_down(Wsrc, Kt, rhs_slabs, evac, u8=False):
        arwar = {}
        nk_r = (Kt + 15) // 16
        for ti in range(4):
            t0 = ti * 512
            rts = {}
            for k in range(Kt):
                dst = AR[k // 4][:, (k % 4) * 512:(k % 4) * 512 + 512]
                rts[k] = b.load(dst, rhs_slabs(k)[:, t0:t0 + 512],
                                deps=arwar.get(k, []))
            arwar = {}
            wwar = [[], [], [], []]
            dqwar = [[], []]
            pswar = {0: [], 1: []}
            for ci in range(16):
                pp = [PA, PB][ci % 2]
                mmt = None
                for r in range(nk_r):
                    wb = WB[(ci % 2) * 2 + r % 2]
                    wbi = (ci % 2) * 2 + r % 2
                    k0 = r * 16
                    kn = min(16, Kt - k0)
                    if u8:
                        wu = WU[ci % 2]
                        lts = []
                        for k in range(kn):
                            lt = b.load(wu[:, k * 128:(k + 1) * 128],
                                        Wsrc[(k0 + k) * 128:(k0 + k + 1) * 128,
                                             ci * 128:(ci + 1) * 128],
                                        deps=dqwar[ci % 2] if k == 0 else [])
                            lts.append(lt)
                        dq = b.act(wb[:, 0:kn * 128], wu[:, 0:kn * 128], AF.Copy,
                                   deps=lts + wwar[wbi], bias=-128.0)
                        dqwar[ci % 2] = [dq]
                        wts = [dq] * kn
                        wwar[wbi] = []
                    else:
                        wb, ctoks, casts = wload(Wsrc, k0 // 128 if False else r * 16,
                                                 kn, ci * 128, 128, ci % 2,
                                                 wwar[ci % 2])
                        wts = ctoks
                        wwar[ci % 2] = list(casts)
                        wbi = ci % 2
                    for k in range(kn):
                        kk = k0 + k
                        mmt = b.mm(pp[0:128, 0:512], wb[:, k * 128:(k + 1) * 128],
                                   AR[kk // 4][:, (kk % 4) * 512:(kk % 4) * 512 + 512],
                                   kk == 0, kk == Kt - 1,
                                   deps=[wts[k], rts[kk]] +
                                        (pswar[ci % 2] if kk == 0 else []))
                        if kk == 0:
                            pswar[ci % 2] = []
                        wwar[wbi].append(mmt)
                        arwar.setdefault(kk, []).append(mmt)
                pswar[ci % 2] = list(evac(ci, ti, t0, pp[0:128, 0:512], mmt))

    gemm_down(wo0t, 32, lambda k: h3t[k],
              (lambda ci, ti, t0, pp, mmt: mk_store_evac(
                  [s0t[i] for i in range(16)], AF.Copy)(ci, ti, t0, 512, pp, mmt)))
    b.barrier()

    f2war = [[], []]

    def ev_f2(ci, ti, t0, pp, mmt):
        k1, ev1, w1 = rEV.get()
        f_ = b.ts(ev1[:], pp, DSCc[:, ci:ci + 1], ALU.mult, deps=[mmt] + w1)
        k2, ev2_, w2 = rEV.get()
        g_ = b.act(ev2_[:], ev1[:], AF.Silu, deps=[f_] + w2)
        lx = b.load(LD2[0][:], xT[ci, :, t0:t0 + 512], deps=f2war[0])
        ls = b.load(LB2[0][:], s0t[ci, :, t0:t0 + 512], deps=f2war[1])
        t_ = b.tt(ev1[:], LB2[0][:], ev2_[:], ALU.mult, deps=[g_, ls])
        x_ = b.tt(ev1[:], ev1[:], LD2[0][:], ALU.add, deps=[t_, lx])
        f2war[0] = [x_]
        f2war[1] = [t_]
        st_ = b.store(x1T[ci, :, t0:t0 + 512], ev1[:], deps=[x_])
        rEV.done(k1, [st_])
        rEV.done(k2, [t_])
        return [f_, x_]
    gemm_down(wd8, 64, lambda k: ht0[k], ev_f2, u8=True)
    b.barrier()

    gw1 = [[], []]
    gw2 = []
    gps_war = []
    g2ps = []
    evg = [[], []]
    for ti in range(4):
        t0 = ti * 512
        mt_a = mt_b = None
        for ci in range(16):
            lt = b.load(LNT[ci % 2][:], x1T[ci, :, t0:t0 + 512], deps=gw1[ci % 2])
            sq = b.act(EV[ci % 2][:], LNT[ci % 2][:], AF.Square, deps=[lt] + evg[ci % 2])
            mt_a = b.mm(PA[0:1, 0:512], ONESC[:], LNT[ci % 2][:], ci == 0, ci == 15,
                        deps=[lt] + (gps_war if ci == 0 else []))
            mt_b = b.mm(PB[0:1, 0:512], ONESC[:], EV[ci % 2][:], ci == 0, ci == 15,
                        deps=[sq])
            gw1[ci % 2] = [mt_a, sq]
            evg[ci % 2] = [mt_b]
        m_ = b.act(MRW[:, 0:512], PA[0:1, 0:512], AF.Copy, deps=[mt_a], scale=1.0 / 2048)
        v_ = b.act(MRW2[:, 0:512], PB[0:1, 0:512], AF.Copy, deps=[mt_b], scale=1.0 / 2048)
        sqm = b.tt(CROW[0:1, 0:512], MRW[:, 0:512], MRW[:, 0:512], ALU.mult, deps=[m_])
        var = b.tt(MRW2[:, 0:512], MRW2[:, 0:512], CROW[0:1, 0:512], ALU.subtract,
                   deps=[v_, sqm])
        sd_ = b.act(MRW2[:, 0:512], MRW2[:, 0:512], AF.Sqrt, deps=[var], bias=1e-5)
        rs_ = b.recip(MRW2[:, 0:512], MRW2[:, 0:512], deps=[sd_])
        bm = b.mm(PA[0:128, 0:512], ONESR[:], MRW[:, 0:512], True, True,
                  deps=[m_, rs_] + gps_war)
        bme = b.act(MBC[:], PA[0:128, 0:512], AF.Copy, deps=[bm] + gps_war)
        br_ = b.mm(PB[0:128, 0:512], ONESR[:], MRW2[:, 0:512], True, True, deps=[rs_])
        bre = b.act(RBC[:], PB[0:128, 0:512], AF.Copy, deps=[br_] + gps_war)
        gps_war = []
        mt2a = mt2b = None
        for ci in range(16):
            lt = b.load(LNT[ci % 2][:], x1T[ci, :, t0:t0 + 512], deps=gw1[ci % 2])
            s1_ = b.tt(EV[2][:], LNT[ci % 2][:], MBC[:], ALU.subtract,
                       deps=[lt, bme] + gw2)
            s2_ = b.tt(EV[2][:], EV[2][:], RBC[:], ALU.mult, deps=[s1_, bre])
            gw1[ci % 2] = [s1_]
            st1 = b.store(xn1f[ci, :, t0:t0 + 512], EV[2][:], deps=[s2_])
            kk, ebuf, war = rEB.get()
            c1_ = b.act(ebuf[:], EV[2][:], AF.Copy, deps=[s2_] + war)
            st2 = b.store(xn1t[ci, :, t0:t0 + 512], ebuf[:], deps=[c1_])
            rEB.done(kk, [st2])
            sq = b.act(EV[3][:], EV[2][:], AF.Square, deps=[s2_] + evg[0])
            mt2a = b.mm(PA[0:1, 0:512], ONESC[:], EV[2][:], ci == 0, ci == 15,
                        deps=[s2_, bme] + (g2ps if ci == 0 else []))
            mt2b = b.mm(PB[0:1, 0:512], ONESC[:], EV[3][:], ci == 0, ci == 15,
                        deps=[sq, bre])
            gw2 = [mt2a, st1, c1_]
            evg[0] = [mt2b]
        m2_ = b.act(MRW[:, 0:512], PA[0:1, 0:512], AF.Copy, deps=[mt2a],
                    scale=1.0 / 2048)
        v2_ = b.act(MRW2[:, 0:512], PB[0:1, 0:512], AF.Copy, deps=[mt2b],
                    scale=1.0 / 2048)
        g2ps = [m2_, v2_]
        sqm2 = b.tt(CROW[0:1, 0:512], MRW[:, 0:512], MRW[:, 0:512], ALU.mult, deps=[m2_])
        var2 = b.tt(MRW2[:, 0:512], MRW2[:, 0:512], CROW[0:1, 0:512], ALU.subtract,
                    deps=[v2_, sqm2])
        sd2 = b.act(MRW2[:, 0:512], MRW2[:, 0:512], AF.Sqrt, deps=[var2], bias=1e-5)
        rs2 = b.recip(MRW2[:, 0:512], MRW2[:, 0:512], deps=[sd2])
        bm2 = b.mm(PA[0:128, 0:512], ONESR[:], MRW[:, 0:512], True, True,
                   deps=[m2_, rs2])
        bme2 = b.act(MBC[:], PA[0:128, 0:512], AF.Copy, deps=[bm2])
        br2 = b.mm(PB[0:128, 0:512], ONESR[:], MRW2[:, 0:512], True, True, deps=[rs2])
        bre2 = b.act(RBC[:], PB[0:128, 0:512], AF.Copy, deps=[br2])
        gps_war = [bme2, bre2]
        for ci in range(16):
            lt = b.load(LNT[ci % 2][:], xn1f[ci, :, t0:t0 + 512], deps=gw1[ci % 2])
            s1_ = b.tt(EV[2][:], LNT[ci % 2][:], MBC[:], ALU.subtract,
                       deps=[lt, bme2] + gw2)
            s2_ = b.tt(EV[2][:], EV[2][:], RBC[:], ALU.mult, deps=[s1_, bre2])
            gw1[ci % 2] = [s1_]
            kk, ebuf, war = rEB.get()
            s3_ = b.ts2(ebuf[:], EV[2][:], LNWc[:, ci:ci + 1], ALU.mult,
                        LNBc[:, ci:ci + 1], ALU.add, deps=[s2_] + war)
            st_ = b.store(hs2t[ci, :, t0:t0 + 512], ebuf[:], deps=[s3_])
            rEB.done(kk, [st_])
            gw2 = [s3_]
    b.barrier()

    for k in range(16):
        b.load(AR[k][:], hs2t[k])
    b.barrier()
    gemm_c(win1, [128 * i for i in range(32)], 128,
           mk_store_evac([zst[i] for i in range(32)], AF.Silu))
    b.barrier()
    gemm_c(win1, [4096 + 128 * i for i in range(48)], 128,
           mk_store_evac([pj1[i] for i in range(48)], AF.Copy))
    b.barrier()
    dt_stage(win1, 10240, 64, DTBC1, ACOL1)
    b.barrier()

    def mk_bias_act_evac(dst_slabs, func, biascols):
        def evac(ci, ti, t0, tw, pp, mmt):
            k, ebuf, war = rEB.get()
            cp = b.act(ebuf[:, 0:tw], pp, func, deps=[mmt] + war,
                       bias=biascols[:, ci:ci + 1])
            st_ = b.store(dst_slabs[ci][:, t0:t0 + tw], ebuf[:, 0:tw], deps=[cp])
            rEB.done(k, [st_])
            return [cp]
        return evac
    gemm_c(fc1, [128 * i for i in range(64)], 128,
           mk_bias_act_evac([h1t[i] for i in range(64)], AF.Gelu_apprx_tanh, FB1c))
    b.barrier()

    def l1_zload(st, c, h):
        hi = h % 2
        tok = b.load(GSL[hi][0:64, 0:256],
                     zst[h // 2][(h % 2) * 64:(h % 2) * 64 + 64,
                                 c * 256:(c + 1) * 256], deps=st['gslwar'][hi])
        st['gslwar'][hi] = []
        return GSL[hi][0:64, 0:256], tok

    def l1_zwar(st, c, h, toks):
        st['gslwar'][h % 2] += toks

    l1_cfg = dict(
        H=64, P=64, N=64, ngrp=16, hpg=4,
        head=lambda g, j: 4 * g + j,
        bsrc=lambda g: pj1[8 + g // 2][(g % 2) * 64:(g % 2) * 64 + 64, :],
        csrc=lambda h: pj1[16 + h // 2][(h % 2) * 64:(h % 2) * 64 + 64, :],
        xsrc=lambda g: pj1[g // 2][(g % 2) * 64:(g % 2) * 64 + 64, :],
        dbc=DBC1, ygp=64,
        ygdst=lambda h: YG[(h % 2) * 64:(h % 2) * 64 + 64,
                           (h // 2) * 256:(h // 2) * 256 + 256],
        zload=l1_zload, zwar=l1_zwar,
        rms=lambda st, c: rms_generic(st, c, 32, 4096.0, 1e-5, NW1c, ynt, [(0, 32)]),
    )
    ssd(l1_cfg)
    b.barrier()

    for k in range(16):
        b.load(AR[k][:], xn1t[k])
    b.barrier()
    swar = []
    for t in range(NT):
        mmt = None
        for k in range(KT):
            mmt = b.mm(PS2[0:1, 0:128], WRB[:, k:k + 1],
                       AR[k][:, t * 128:(t + 1) * 128], k == 0, k == KT - 1,
                       deps=(swar if k == 0 else []))
        cp = b.act(SROW[0:1, t * 128:(t + 1) * 128], PS2[0:1, 0:128], AF.Copy,
                   deps=[mmt])
        swar = [cp]
    brt = b.ts(SROW[:], SROW[:], BRC[0:1, 0:1], ALU.add, deps=swar + [tbr])
    mx = b.red(SC[3][:], SROW[:], ALU.max, deps=[brt])
    hi_ = b.ts(SC[1][:], SC[3][:], 1.0, ALU.add, deps=[mx])
    ng = b.ts(CROW[:], SROW[:], -1.0, ALU.mult, deps=[brt])
    mn = b.red(SC[3][:], CROW[:], ALU.max, deps=[ng, hi_])
    lo_ = b.ts2(SC[0][:], SC[3][:], -1.0, ALU.mult, -1.0, ALU.add, deps=[mn])
    prev = [lo_, hi_]
    for it in range(30):
        t1_ = b.tt(SC[2][:], SC[0][:], SC[1][:], ALU.add, deps=prev)
        t2_ = b.ts(SC[2][:], SC[2][:], 0.5, ALU.mult, deps=[t1_])
        c_ = b.ts(CROW[:], SROW[:], SC[2][0:1, 0:1], ALU.is_gt, deps=[t2_])
        r_ = b.red(SC[3][:], CROW[:], ALU.add, deps=[c_])
        p_ = b.ts(SC[4][:], SC[3][:], 1024.0, ALU.is_ge, deps=[r_])
        d1_ = b.tt(SC[5][:], SC[2][:], SC[0][:], ALU.subtract, deps=[p_])
        d2_ = b.tt(SC[5][:], SC[5][:], SC[4][:], ALU.mult, deps=[d1_])
        lo2 = b.tt(SC[0][:], SC[0][:], SC[5][:], ALU.add, deps=[d2_])
        e1_ = b.tt(SC[6][:], SC[1][:], SC[2][:], ALU.subtract, deps=[p_])
        e2_ = b.tt(SC[6][:], SC[6][:], SC[4][:], ALU.mult, deps=[e1_])
        hi2 = b.tt(SC[1][:], SC[2][:], SC[6][:], ALU.add, deps=[e2_, lo2])
        prev = [lo2, hi2]
    mrow = b.ts(CROW[:], SROW[:], SC[0][0:1, 0:1], ALU.is_gt, deps=prev)
    mwar = []
    for t4 in range(4):
        bmm = b.mm(PA[0:128, 0:512], ONESR[:], CROW[0:1, t4 * 512:(t4 + 1) * 512],
                   True, True, deps=[mrow] + mwar)
        mc = b.act(MASKB[:, t4 * 512:(t4 + 1) * 512], PA[0:128, 0:512], AF.Copy,
                   deps=[bmm])
        mwar = [mc]
    b.barrier()

    gemm_c(w1m, [128 * i for i in range(64)], 128,
           mk_bias_act_evac([hmt[i] for i in range(64)], AF.Gelu_apprx_tanh, B1Mc))
    b.barrier()

    gemm_down(wo1, 32, lambda k: ynt[k],
              (lambda ci, ti, t0, pp, mmt: mk_store_evac(
                  [att[i] for i in range(16)], AF.Copy)(ci, ti, t0, 512, pp, mmt)))
    b.barrier()

    j2war = [[], []]

    def ev_s1(ci, ti, t0, pp, mmt):
        k1, ev1, w1 = rEV.get()
        c1_ = b.ts(ev1[:], pp, FB2c[:, ci:ci + 1], ALU.add, deps=[mmt] + w1)
        la = b.load(LB2[0][:], att[ci, :, t0:t0 + 512], deps=j2war[0])
        lx = b.load(LB2[1][:], xn1t[ci, :, t0:t0 + 512], deps=j2war[1])
        a1_ = b.tt(ev1[:], ev1[:], LB2[0][:], ALU.add, deps=[c1_, la])
        a2_ = b.tt(ev1[:], ev1[:], LB2[1][:], ALU.add, deps=[a1_, lx])
        j2war[0] = [a1_]
        j2war[1] = [a2_]
        kk, ebuf, war = rEB.get()
        c2_ = b.act(ebuf[:], ev1[:], AF.Copy, deps=[a2_] + war)
        st_ = b.store(s1t[ci, :, t0:t0 + 512], ebuf[:], deps=[c2_])
        rEB.done(kk, [st_])
        rEV.done(k1, [c2_])
        return [c1_, c2_]
    gemm_down(fc2, 64, lambda k: h1t[k], ev_s1)
    b.barrier()

    k3war = [[], [], [], []]

    def ev_k3(ci, ti, t0, pp, mmt):
        k1, ev1, w1 = rEV.get()
        c1_ = b.ts(ev1[:], pp, B2Mc[:, ci:ci + 1], ALU.add, deps=[mmt] + w1)
        mk = b.tt(ev1[:], ev1[:], MASKB[:, t0:t0 + 512], ALU.mult, deps=[c1_])
        g_ = b.act(ev1[:], ev1[:], AF.Silu, deps=[mk])
        ls = b.load(LB2[2][:], s1t[ci, :, t0:t0 + 512], deps=k3war[0])
        lx = b.load(LD2[1][:], x1T[ci, :, t0:t0 + 512], deps=k3war[1])
        m1_ = b.tt(ev1[:], ev1[:], LB2[2][:], ALU.mult, deps=[g_, ls])
        x2_ = b.tt(ev1[:], ev1[:], LD2[1][:], ALU.add, deps=[m1_, lx])
        k3war[0] = [m1_]
        k3war[1] = [x2_]
        toks = [x2_]
        for j in range(4):
            pt = [PT0, PT1][j % 2]
            tr = b.trans(pt[:], ev1[:, j * 128:(j + 1) * 128], IDF[:],
                         deps=[x2_] + k3war[2 + j % 2])
            k2, ev2_, w2 = rEV.get()
            cp = b.act(ev2_[:, 0:128], pt[:], AF.Copy, deps=[tr] + w2)
            st_ = b.store(out_d[t0 + j * 128:t0 + (j + 1) * 128,
                                ci * 128:(ci + 1) * 128], ev2_[:, 0:128], deps=[cp])
            rEV.done(k2, [st_])
            k3war[2 + j % 2] = [cp]
            toks.append(tr)
        rEV.done(k1, toks[1:])
        return toks
    gemm_down(w2m, 64, lambda k: hmt[k], ev_k3)
    b.barrier()
    b.memset(SC[0][:], 0.0, deps=[('st', b.cnt['st'])])

    block = es.enter_context(nc.Block())

    @block.sync
    def _(eng):
        b.replay('ld', eng, sems)

    @block.gpsimd
    def _(eng):
        b.replay('wld', eng, sems)

    @block.tensor
    def _(eng):
        b.replay('pe', eng, sems)

    @block.scalar
    def _(eng):
        b.replay('act', eng, sems)

    @block.vector
    def _(eng):
        b.replay('dve', eng, sems)

    es.close()
    return nc


def kernel(**inputs):
    bf = ml_dtypes.bfloat16
    I = inputs

    def u8view(a):
        return np.ascontiguousarray(a.view(np.uint8)[:, ::4])

    com = {
        "w0t": np.ascontiguousarray(I['L0_attn_in_proj_w'].T),
        "cw": np.ascontiguousarray(I['L0_attn_conv_w'].reshape(6144, 4)),
        "cb": I['L0_attn_conv_b'],
        "dtb0": I['L0_attn_dt_bias'],
        "alog0": I['L0_attn_A_log'],
        "dd0": I['L0_attn_D'],
        "nw0": I['L0_attn_norm_w'],
        "wo0t": np.ascontiguousarray(I['L0_attn_out_proj_w'].T),
        "wg8": np.ascontiguousarray(u8view(I['L0_ffn_gate_w']).T),
        "wu8": np.ascontiguousarray(u8view(I['L0_ffn_up_w']).T),
        "wd8": np.ascontiguousarray(u8view(I['L0_ffn_down_w']).T),
        "gsc": I['L0_ffn_gate_scale'],
        "usc": I['L0_ffn_up_scale'],
        "dsc": I['L0_ffn_down_scale'],
        "lnw1": I['L1_attn_ln_w'],
        "lnb1": I['L1_attn_ln_b'],
        "win1": I['L1_attn_W_in'],
        "dtb1": I['L1_attn_dt_bias'],
        "alog1": I['L1_attn_A_log'],
        "dd1": I['L1_attn_D'],
        "nw1": I['L1_attn_norm_w'],
        "wo1": I['L1_attn_W_out'],
        "fc1": I['L1_attn_fc1_w'],
        "fb1": I['L1_attn_fc1_b'],
        "fc2": I['L1_attn_fc2_w'],
        "fb2": I['L1_attn_fc2_b'],
        "wr": I['L1_ffn_w_r'],
        "br": np.asarray(I['L1_ffn_b_r'], np.float32).reshape(1),
        "w1m": I['L1_ffn_w1'],
        "b1m": I['L1_ffn_b1'],
        "w2m": I['L1_ffn_w2'],
        "b2m": I['L1_ffn_b2'],
        "idf": np.eye(128, dtype=np.float32),
        "idb": np.eye(128, dtype=np.float32).astype(bf),
        "triu": np.triu(np.ones((128, 128), np.float32)),
        "lowm": np.where(np.arange(128)[:, None] > np.arange(128)[None, :],
                         np.float32(-88.0), np.float32(0.0)).astype(np.float32),
    }
    com = {k: (np.ascontiguousarray(v) if isinstance(v, np.ndarray) else v)
           for k, v in com.items()}
    x = I['L0_attn_x']
    nc = build_program()
    in_maps = []
    for core in range(8):
        m = dict(com)
        m["x"] = np.ascontiguousarray(x[core % 4])
        in_maps.append(m)
    res = run_bass_kernel_spmd(nc, in_maps, core_ids=list(range(8)))
    return np.stack([res.results[bb]["out"] for bb in range(4)], 0)


